# version 03
# baseline (speedup 1.0000x reference)
import numpy as np
import concourse.bass as bass
import concourse.mybir as mybir
from concourse.bass_utils import run_bass_kernel_spmd

F32 = mybir.dt.float32
BF16 = mybir.dt.bfloat16
FP8 = mybir.dt.float8e4
I32 = mybir.dt.int32
U8 = mybir.dt.uint8
ALU = mybir.AluOpType
ACT = mybir.ActivationFunctionType

B, S, D = 4, 2048, 2048
T = 1024
NCH = 16
MLP = 16384
MLPS = 2048
PAIRS = [[0, 1], [2, 3], [4, 5], [6, 7]]
ALL8 = [[0, 1, 2, 3, 4, 5, 6, 7]]
ALL8 = [[0, 1, 2, 3, 4, 5, 6, 7]]
ENGS = ("sync", "vector", "scalar", "tensor", "gpsimd")


def _perm(n):
    out = np.empty(n, np.int64)
    for cc in range(n // 256):
        out[256 * cc:256 * cc + 128] = 256 * cc + 2 * np.arange(128)
        out[256 * cc + 128:256 * cc + 256] = 256 * cc + 2 * np.arange(128) + 1
    return out


PERM = _perm(D)
PERMN = _perm(MLP)


def _u8view(w):
    return np.ascontiguousarray(w.view(np.uint8).reshape(*w.shape, 4)[..., 0])


class BP:

    def __init__(self, nc):
        self.nc = nc
        self.q = {e: [] for e in ENGS}
        self.cnt = {}
        self.h = {}

    def sem(self, name):
        if name not in self.h:
            self.h[name] = self.nc.alloc_semaphore(name)
            self.cnt[name] = 0
        return self.h[name]

    def wait(self, eng, name, val=None, drain=False):
        hh = self.sem(name)
        v = self.cnt[name] if val is None else val
        if v > 0:
            self.q[eng].append(lambda e, hh=hh, v=v: e.wait_ge(hh, v))
            if drain:
                self.q[eng].append(lambda e: e.nop(cycle_cnt=512))

    def emit(self, eng, maker, inc=None, n=1):
        if inc is not None:
            hh = self.sem(inc)
            self.cnt[inc] += n
            self.q[eng].append(lambda e, maker=maker, hh=hh, n=n: maker(e).then_inc(hh, n))
        else:
            self.q[eng].append(lambda e, maker=maker: maker(e))
        return self.cnt.get(inc)

    def val(self, name):
        self.sem(name)
        return self.cnt[name]


class MK:

    _dbgq = False

    def __init__(self):
        self.nc = bass.Bass("TRN2", target_bir_lowering=False, debug=False)
        self.bp = BP(self.nc)
        self._mk_dram()
        self._mk_sbuf()
        self._mk_psum()

    def _mk_dram(self):
        nc = self.nc

        def din(name, shape, dtype=F32):
            return nc.dram_tensor(name, shape, dtype, kind="ExternalInput").ap()

        def dint(name, shape, dtype=F32):
            return nc.dram_tensor(name, shape, dtype, kind="Internal").ap()

        self.x0T = din("x0T", [128, NCH * T])
        self.wq = din("wq", [128, NCH * D]); self.wk = din("wk", [128, NCH * 512])
        self.wv = din("wv", [128, NCH * 512])
        self.wphi = din("wphi", [128, NCH * D]); self.wo = din("wo", [128, NCH * D])
        self.in2 = din("in2", [128, NCH * D]); self.sc01 = din("sc01", [1, 2])
        self.wqkv = din("wqkv", [128, NCH * 6144]); self.wout = din("wout", [128, NCH * D])
        self.gpk = din("gpk", [1024, MLPS], U8); self.gsc = din("gsc", [1024, MLPS])
        self.upk = din("upk", [1024, MLPS], U8); self.usc = din("usc", [1024, MLPS])
        self.dpk = din("dpk", [1024, D], U8); self.dsc = din("dsc", [1024, D])
        self.ropeC = din("ropeC", [128, 512]); self.ropeS = din("ropeS", [128, 512])
        self.ident = din("ident", [128, 128], BF16)
        self.identf = din("identf", [128, 128])

        self.out = self.nc.dram_tensor("out", [T, D], F32, kind="ExternalOutput").ap()

        self.xphi_sp = dint("xphi_sp", [D, T])
        self.qkt_sp = dint("qkt_sp", [2 * 16 * 8 * 128, 128], BF16)
        self.v_sp = dint("v_sp", [2 * T, 512], BF16)
        self.ctx_sp = dint("ctx_sp", [2 * D, T], BF16)
        self.q_sp = dint("q_sp", [2 * D, T], BF16)
        self.comb_sp = dint("comb_sp", [2 * D, T], BF16)
        self.wgd = dint("wgd", [2 * D, MLPS], BF16)
        self.wud = dint("wud", [2 * D, MLPS], BF16)
        self.wdd = dint("wdd", [2 * MLPS, D], BF16)
        self.ar_qg_i = dint("ar_qg_i", [1, D]); self.ar_qg_o = dint("ar_qg_o", [1, D])
        self.ar_mx_i = dint("ar_mx_i", [1, 32]); self.ar_mx_o = dint("ar_mx_o", [2, 32])
        self.ar_oz_i = dint("ar_oz_i", [64, 2080]); self.ar_oz_o = dint("ar_oz_o", [64, 2080])
        self.kv_in = [dint(f"kv_in{qq}", [8192, 256], BF16) for qq in range(4)]
        self.kv_out = [dint(f"kv_out{qq}", [16384, 256], BF16) for qq in range(4)]
        self.xg_in = [dint(f"xg_in{hh}", [2 * D, 512], BF16) for hh in range(2)]
        self.xg_out = [dint(f"xg_out{hh}", [16 * D, 512], BF16) for hh in range(2)]
        self.rs_in = [dint(f"rs_in{hh}", [8 * D, 512]) for hh in range(2)]
        self.rs_out = [dint(f"rs_out{hh}", [D, 512]) for hh in range(2)]

    def _mk_sbuf(self):
        nc = self.nc
        self.xres = nc.alloc_sbuf_tensor("xres", [128, NCH * T], F32)
        self.arA = nc.alloc_sbuf_tensor("arA", [128, 16384], BF16)
        self.arB = nc.alloc_sbuf_tensor("arB", [128, 16384], BF16)
        self.arC = nc.alloc_sbuf_tensor("arC", [128, 4096], F32)
        self.arD = nc.alloc_sbuf_tensor("arD", [128, 8192], BF16)
        self.ar8 = nc.alloc_sbuf_tensor("ar8", [128, 8192], FP8)
        self.scr = nc.alloc_sbuf_tensor("scr", [128, 1536], F32)
        self.rows = nc.alloc_sbuf_tensor("rows", [1, 3584], F32)
        self.bca0 = nc.alloc_sbuf_tensor("bca0", [128, 512], F32)
        self.bcm = nc.alloc_sbuf_tensor("bcm", [128, 512], F32)
        self.ropc = nc.alloc_sbuf_tensor("ropc", [128, 512], F32)
        self.rops = nc.alloc_sbuf_tensor("rops", [128, 512], F32)
        self.qtmp = nc.alloc_sbuf_tensor("qtmp", [128, 1024], F32)
        self.vt1 = nc.alloc_sbuf_tensor("vt1", [128, 512], BF16)
        self.vt2 = nc.alloc_sbuf_tensor("vt2", [128, 512], BF16)
        self.qspl1 = nc.alloc_sbuf_tensor("qspl1", [128, 256], BF16)
        self.qspl2 = nc.alloc_sbuf_tensor("qspl2", [128, 256], BF16)
        self.qtb = nc.alloc_sbuf_tensor("qtb", [128, 128], BF16)
        self.prowb = nc.alloc_sbuf_tensor("prowb", [128, 256], F32)
        self.dqb = nc.alloc_sbuf_tensor("dqb", [128, 4], F32)
        self.co = nc.alloc_sbuf_tensor("co", [128, 16], F32)
        self.onec = nc.alloc_sbuf_tensor("onec", [128, 1], F32)
        self.onecb = nc.alloc_sbuf_tensor("onecb", [128, 1], BF16)
        self.oner = nc.alloc_sbuf_tensor("oner", [1, 128], F32)
        self.idn = nc.alloc_sbuf_tensor("idn", [128, 128], BF16)
        self.idnf = nc.alloc_sbuf_tensor("idnf", [128, 128], F32)
        self.arCi = nc.virtual_sbuf_tensor("arCi", [128, 4096], I32)
        nc.back_virtual_tensor_with(self.arCi, self.arC)
        self.ar8u = nc.virtual_sbuf_tensor("ar8u", [128, 8192], U8)
        nc.back_virtual_tensor_with(self.ar8u, self.ar8)

    def _mk_psum(self):
        nc = self.nc
        self.pA = nc.alloc_psum_tensor("pA", [128, 512], F32)
        self.pB = nc.alloc_psum_tensor("pB", [128, 512], F32)
        self.pC = nc.alloc_psum_tensor("pC", [128, 512], F32)
        self.pD = nc.alloc_psum_tensor("pD", [128, 512], F32)
        self.pE = nc.alloc_psum_tensor("pE", [128, 512], F32)
        self.pF = nc.alloc_psum_tensor("pF", [128, 512], F32)
        self.pT = nc.alloc_psum_tensor("pT", [128, 1024], BF16)
        self.pG = nc.alloc_psum_tensor("pG", [128, 512], F32)
        self.pR = self.pG[0:1, 0:512]
        self.pZg = self.pG[:, 0:16]

    C_EPS, C_I2048, C_HALF, C_SEQ, C_ISQ, C_PI, C_ONE, C_797, C_I128, C_SPARE = range(10)
    R_SUM, R_SQ, R_TMP, R_QG = 0, 512, 1024, 1536

    def CO(self, i):
        return self.co[:, i:i + 1]

    def CO0(self, i):
        return self.co[0:1, i:i + 1]

    def brow(self, row_ap, width, dst_ap, bubble=False, ps=None):
        bp, pC, oner = self.bp, (self.pC if ps is None else ps), self.oner
        bp.wait("tensor", "ev")
        bp.emit("tensor", lambda e, row_ap=row_ap, width=width: e.matmul(
            pC[:, 0:width], oner[0:1, :], row_ap, start=True, stop=True), inc="mm")
        bp.wait("vector", "mm", drain=True)
        bp.emit("vector", lambda e, dst_ap=dst_ap, width=width: e.tensor_copy(
            out=dst_ap, in_=pC[:, 0:width]), inc="ev")
        if bubble:
            bp.emit("vector", lambda e: e.nop(cycle_cnt=4000))

    def vbubble(self):
        self.bp.emit("vector", lambda e: e.nop(cycle_cnt=4000))

    def ln_stats(self, t0, tw, rmul_ap=None):
        bp = self.bp
        scr, rows, xres = self.scr, self.rows, self.xres
        pR, onec = self.pR, self.onec
        R_SUM, R_SQ, R_TMP = self.R_SUM, self.R_SQ, self.R_TMP
        acc = scr[:, 0:512]; acc2 = scr[:, 512:1024]; sq = scr[:, 1024:1536]
        bp.wait("vector", "mm")
        for k in range(NCH):
            xk = xres[:, k * T + t0: k * T + t0 + tw]
            if k == 0:
                bp.emit("vector", lambda e, xk=xk, acc=acc, tw=tw: e.tensor_copy(out=acc[:, 0:tw], in_=xk), inc="ev")
            else:
                bp.emit("vector", lambda e, xk=xk, acc=acc, tw=tw: e.tensor_add(out=acc[:, 0:tw], in0=acc[:, 0:tw], in1=xk), inc="ev")
            bp.wait("scalar", "ev")
            bp.emit("scalar", lambda e, xk=xk, sq=sq, tw=tw: e.activation(out=sq[:, 0:tw], in_=xk, func=ACT.Square), inc="sc")
            bp.wait("vector", "sc")
            if k == 0:
                bp.emit("vector", lambda e, acc2=acc2, sq=sq, tw=tw: e.tensor_copy(out=acc2[:, 0:tw], in_=sq[:, 0:tw]), inc="ev")
            else:
                bp.emit("vector", lambda e, acc2=acc2, sq=sq, tw=tw: e.tensor_add(out=acc2[:, 0:tw], in0=acc2[:, 0:tw], in1=sq[:, 0:tw]), inc="ev")
        bp.wait("tensor", "ev")
        bp.emit("tensor", lambda e, acc=acc, tw=tw: e.matmul(pR[0:1, 0:tw], onec[:, :], acc[:, 0:tw], start=True, stop=True), inc="mm")
        bp.wait("vector", "mm", drain=True)
        bp.emit("vector", lambda e, tw=tw: e.tensor_copy(out=rows[0:1, R_SUM:R_SUM + tw], in_=pR[0:1, 0:tw]), inc="ev")
        bp.wait("tensor", "ev")
        bp.emit("tensor", lambda e, acc2=acc2, tw=tw: e.matmul(pR[0:1, 0:tw], onec[:, :], acc2[:, 0:tw], start=True, stop=True), inc="mm")
        bp.wait("vector", "mm", drain=True)
        bp.emit("vector", lambda e, tw=tw: e.tensor_copy(out=rows[0:1, R_SQ:R_SQ + tw], in_=pR[0:1, 0:tw]), inc="ev")
        bp.emit("vector", lambda e, tw=tw: e.tensor_single_scalar(
            out=rows[0:1, R_SUM:R_SUM + tw], in_=rows[0:1, R_SUM:R_SUM + tw], scalar=self.CO0(self.C_I2048), op=ALU.mult))
        bp.emit("vector", lambda e, tw=tw: e.tensor_mul(
            out=rows[0:1, R_TMP:R_TMP + tw], in0=rows[0:1, R_SUM:R_SUM + tw], in1=rows[0:1, R_SUM:R_SUM + tw]))
        bp.emit("vector", lambda e, tw=tw: e.scalar_tensor_tensor(
            out=rows[0:1, R_SQ:R_SQ + tw], in0=rows[0:1, R_SQ:R_SQ + tw], scalar=1.0 / 2048.0,
            in1=rows[0:1, R_TMP:R_TMP + tw], op0=ALU.mult, op1=ALU.subtract))
        bp.emit("vector", lambda e, tw=tw: e.tensor_single_scalar(
            out=rows[0:1, R_SQ:R_SQ + tw], in_=rows[0:1, R_SQ:R_SQ + tw], scalar=self.CO0(self.C_EPS), op=ALU.add), inc="ev")
        bp.wait("scalar", "ev")
        bp.emit("scalar", lambda e, tw=tw: e.activation(out=rows[0:1, R_TMP:R_TMP + tw],
                                                        in_=rows[0:1, R_SQ:R_SQ + tw], func=ACT.Sqrt), inc="sc")
        bp.wait("vector", "sc")
        bp.emit("vector", lambda e, tw=tw: e.reciprocal(out=rows[0:1, R_SQ:R_SQ + tw],
                                                        in_=rows[0:1, R_TMP:R_TMP + tw]), inc="ev")
        if rmul_ap is not None:
            bp.emit("vector", lambda e, tw=tw, rmul_ap=rmul_ap: e.tensor_single_scalar(
                out=rows[0:1, R_SQ:R_SQ + tw], in_=rows[0:1, R_SQ:R_SQ + tw], scalar=rmul_ap, op=ALU.mult), inc="ev")
        bp.emit("vector", lambda e, tw=tw: e.tensor_mul(
            out=rows[0:1, R_SUM:R_SUM + tw], in0=rows[0:1, R_SUM:R_SUM + tw],
            in1=rows[0:1, R_SQ:R_SQ + tw]), inc="ev")
        self.brow(rows[0:1, R_SQ:R_SQ + tw], tw, self.bca0[:, 0:tw])
        self.brow(rows[0:1, R_SUM:R_SUM + tw], tw, self.bcm[:, 0:tw])

    def ln_apply(self, k, t0, tw, out_ap, tmp_ap):
        bp, xres = self.bp, self.xres
        xk = xres[:, k * T + t0: k * T + t0 + tw]
        bp.emit("vector", lambda e, xk=xk, tmp_ap=tmp_ap, tw=tw: e.tensor_mul(
            out=tmp_ap, in0=xk, in1=self.bca0[:, 0:tw]), inc="ev")
        bp.emit("vector", lambda e, out_ap=out_ap, tmp_ap=tmp_ap, tw=tw: e.tensor_sub(
            out=out_ap, in0=tmp_ap, in1=self.bcm[:, 0:tw]), inc="ev")

    def p0(self):
        bp = self.bp
        for i, v in [(self.C_EPS, 1e-5), (self.C_I2048, 1.0 / 2048.0), (self.C_HALF, 0.5),
                     (self.C_SEQ, 2048.0), (self.C_ISQ, float(1.0 / np.sqrt(128.0))),
                     (self.C_PI, float(np.pi)), (self.C_ONE, 1.0),
                     (self.C_797, 0.7978845608028654)]:
            bp.emit("vector", lambda e, i=i, v=v: e.memset(self.co[:, i:i + 1], v))
        bp.emit("vector", lambda e: e.memset(self.onec[:, :], 1.0))
        bp.emit("vector", lambda e: e.memset(self.onecb[:, :], 1.0))
        bp.emit("vector", lambda e: e.memset(self.rows[0:1, self.R_QG:self.R_QG + 2048], 0.0))
        bp.emit("vector", lambda e: e.memset(self.oner[:, :], 1.0), inc="ev")

        bp.emit("sync", lambda e: e.dma_start(out=self.xres[:, :], in_=self.x0T[:, :]), inc="ld", n=16)
        bp.emit("sync", lambda e: e.dma_start(
            out=self.rows[0:1, self.R_TMP:self.R_TMP + 2], in_=self.sc01[0:1, 0:2]), inc="ld", n=16)
        bp.emit("sync", lambda e: e.dma_start(out=self.ropc[:, :], in_=self.ropeC[:, :]), inc="ld", n=16)
        bp.emit("sync", lambda e: e.dma_start(out=self.rops[:, :], in_=self.ropeS[:, :]), inc="ld", n=16)
        bp.emit("sync", lambda e: e.dma_start(out=self.idn[:, :], in_=self.ident[:, :]), inc="ld", n=16)
        bp.emit("sync", lambda e: e.dma_start(out=self.idnf[:, :], in_=self.identf[:, :]), inc="ld", n=16)

        bp.wait("vector", "ld")
        R_TMP = self.R_TMP
        bp.emit("vector", lambda e: e.reciprocal(out=self.rows[0:1, R_TMP + 2:R_TMP + 4],
                                                 in_=self.rows[0:1, R_TMP:R_TMP + 2]))
        self.vbubble()
        bp.emit("vector", lambda e: e.tensor_mul(out=self.rows[0:1, R_TMP + 4:R_TMP + 5],
                                                 in0=self.rows[0:1, R_TMP + 2:R_TMP + 3],
                                                 in1=self.rows[0:1, R_TMP + 3:R_TMP + 4]), inc="ev")
        self.vbubble()
        self.brow(self.rows[0:1, R_TMP + 4:R_TMP + 5], 1, self.dqb[:, 0:1])
        self.brow(self.rows[0:1, R_TMP:R_TMP + 1], 1, self.dqb[:, 1:2])
        self.brow(self.rows[0:1, R_TMP + 1:R_TMP + 2], 1, self.dqb[:, 2:3])

    def p1a_half(self, th):
        bp = self.bp
        t0 = 512 * th
        x1v = self.arA[:, 0:8192]
        x2v = self.arA[:, 8192:16384]
        wsp = [(self.arB[:, 0:4096], self.arB[:, 4096:8192]),
               (self.arB[:, 8192:12288], self.arB[:, 12288:16384])]
        wstg = [self.arC[:, 0:2048], self.arC[:, 2048:4096]]

        self.ln_stats(t0, 512)
        for k in range(NCH):
            self.ln_apply(k, t0, 512, self.scr[:, 0:512], self.scr[:, 0:512])
            x1k = x1v[:, 512 * k:512 * k + 512]
            x2k = x2v[:, 512 * k:512 * k + 512]
            bp.emit("vector", lambda e, x1k=x1k: e.tensor_copy(out=x1k, in_=self.scr[:, 0:512]), inc="ev")
            bp.emit("vector", lambda e, x1k=x1k, x2k=x2k: e.tensor_sub(
                out=x2k, in0=self.scr[:, 0:512], in1=x1k), inc="ev")

        split_done = [0] * 24
        mm_done = [0] * 12
        for g in range(12):
            pp = g % 2
            if g < 8:
                src, c0 = self.wq, 256 * g
            elif g < 10:
                src, c0 = self.wk, 256 * (g - 8)
            else:
                src, c0 = self.wv, 256 * (g - 10)
            for kh in range(2):
                sb = wstg[(2 * g + kh) % 2]
                if 2 * g + kh >= 2:
                    bp.wait("sync", "ev", split_done[2 * g + kh - 2])
                ncols = {True: D, False: 512}[g < 8]
                srcv = src.rearrange("p (k c) -> p k c", k=NCH)
                bp.emit("sync", lambda e, srcv=srcv, c0=c0, kh=kh, sb=sb: e.dma_start(
                    out=sb.rearrange("p (k c) -> p k c", k=8),
                    in_=srcv[:, 8 * kh:8 * kh + 8, c0:c0 + 256]),
                    inc="ld", n=16)
                bp.wait("vector", "ld")
                if g >= 2 and kh == 0:
                    bp.wait("vector", "mm", mm_done[g - 2])
                d1 = wsp[pp][0][:, 2048 * kh:2048 * kh + 2048]
                d2 = wsp[pp][1][:, 2048 * kh:2048 * kh + 2048]
                bp.emit("vector", lambda e, sb=sb, d1=d1: e.tensor_copy(out=d1, in_=sb), inc="ev")
                bp.emit("vector", lambda e, sb=sb, d1=d1, d2=d2: e.tensor_sub(out=d2, in0=sb, in1=d1), inc="ev")
                split_done[2 * g + kh] = bp.val("ev")

            bp.wait("tensor", "ev", split_done[2 * g + 1])
            for tg in range(4):
                tgg = 4 * th + tg
                ps = self.pA if tg % 2 == 0 else self.pB
                bp.wait("tensor", "ev")
                def mk_mms(e, pp=pp, tg=tg, ps=ps):
                    last = None
                    for k in range(NCH):
                        xs1 = x1v[:, 512 * k + 128 * tg: 512 * k + 128 * tg + 128]
                        xs2 = x2v[:, 512 * k + 128 * tg: 512 * k + 128 * tg + 128]
                        ws1 = wsp[pp][0][:, 256 * k:256 * k + 256]
                        ws2 = wsp[pp][1][:, 256 * k:256 * k + 256]
                        e.matmul(ps[:, 0:256], xs1, ws1, start=(k == 0), stop=False)
                        e.matmul(ps[:, 0:256], xs2, ws1, start=False, stop=False)
                        last = e.matmul(ps[:, 0:256], xs1, ws2, start=False, stop=(k == NCH - 1))
                    return last
                bp.emit("tensor", mk_mms, inc="mm")
                mm_done[g] = bp.val("mm")
                self._p1a_evac(g, tgg, ps)

        phi_mm = [0] * 16
        for fo in range(NCH):
            pp = fo % 2
            for kh in range(2):
                sb = wstg[(2 * fo + kh) % 2]
                bp.wait("sync", "ev")
                wphiv = self.wphi.rearrange("p (k c) -> p k c", k=NCH)
                bp.emit("sync", lambda e, fo=fo, kh=kh, sb=sb, wphiv=wphiv: e.dma_start(
                    out=sb[:, 0:1024].rearrange("p (k c) -> p k c", k=8),
                    in_=wphiv[:, 8 * kh:8 * kh + 8, 128 * fo:128 * fo + 128]),
                    inc="ld", n=16)
                bp.wait("vector", "ld")
                if fo >= 2 and kh == 0:
                    bp.wait("vector", "mm", phi_mm[fo - 2])
                d1 = wsp[pp][0][:, 1024 * kh:1024 * kh + 1024]
                d2 = wsp[pp][1][:, 1024 * kh:1024 * kh + 1024]
                s8 = sb[:, 0:1024]
                bp.emit("vector", lambda e, s8=s8, d1=d1: e.tensor_copy(out=d1, in_=s8), inc="ev")
                bp.emit("vector", lambda e, s8=s8, d1=d1, d2=d2: e.tensor_sub(out=d2, in0=s8, in1=d1), inc="ev")
            ev_spl = bp.val("ev")
            bp.wait("tensor", "ev", ev_spl)
            def mk_phi(e, pp=pp):
                last = None
                for k in range(NCH):
                    w1 = wsp[pp][0][:, 128 * k:128 * k + 128]
                    w2 = wsp[pp][1][:, 128 * k:128 * k + 128]
                    xs1 = x1v[:, 512 * k:512 * k + 512]
                    xs2 = x2v[:, 512 * k:512 * k + 512]
                    e.matmul(self.pE[:, 0:512], w1, xs1, start=(k == 0), stop=False)
                    e.matmul(self.pE[:, 0:512], w2, xs1, start=False, stop=False)
                    last = e.matmul(self.pE[:, 0:512], w1, xs2, start=False, stop=(k == NCH - 1))
                return last
            bp.wait("tensor", "st")
            bp.emit("tensor", mk_phi, inc="mm")
            phi_mm[fo] = bp.val("mm")
            bp.wait("vector", "mm", drain=True)
            bp.wait("vector", "st")
            bp.emit("vector", lambda e: e.tensor_copy(out=self.qtmp[:, 0:512], in_=self.pE[:, 0:512]), inc="ev")
            bp.wait("sync", "ev")
            bp.emit("sync", lambda e, fo=fo, t0=t0: e.dma_start(
                out=self.xphi_sp[128 * fo:128 * fo + 128, t0:t0 + 512], in_=self.qtmp[:, 0:512]),
                inc="st", n=16)

    def _p1a_evac(self, g, tgg, ps):
        bp = self.bp
        if g < 8:
            bp.wait("vector", "mm", drain=True)
            self._rope_kappa(ps, tgg, self.qtmp[:, 0:256], self.qtmp[:, 256:1024])
            bp.wait("tensor", "ev")
            bp.emit("tensor", lambda e: e.matmul(self.pR[0:1, 0:256], self.onec[:, :],
                                                 self.qtmp[:, 0:256], start=True, stop=True), inc="mm")
            bp.wait("vector", "mm", drain=True)
            qg = self.rows[0:1, self.R_QG + 256 * g: self.R_QG + 256 * g + 256]
            bp.emit("vector", lambda e, qg=qg: e.tensor_add(out=qg, in0=qg, in1=self.pR[0:1, 0:256]), inc="ev")
            bp.emit("vector", lambda e: e.tensor_copy(out=self.qspl1[:, :], in_=self.qtmp[:, 0:256]), inc="ev")
            bp.emit("vector", lambda e: e.tensor_sub(out=self.qspl2[:, :], in0=self.qtmp[:, 0:256],
                                                     in1=self.qspl1[:, :]), inc="ev")
            if self._dbgq and g == 0 and tgg == 0:
                bp.emit("vector", lambda e: e.tensor_copy(out=self.prowb[:, :], in_=self.qspl1[:, :]), inc="ev")
            self.vbubble()
            ev_spl = bp.val("ev")
            for sp, srcb in ((0, self.qspl1), (1, self.qspl2)):
                for qc2 in range(2):
                    qc = 2 * g + qc2
                    bp.wait("tensor", "ev")
                    bp.emit("tensor", lambda e, srcb=srcb, qc2=qc2: e.transpose(
                        self.pT[:, 0:128], srcb[:, 128 * qc2:128 * qc2 + 128], self.idn[:, :]), inc="mm")
                    bp.wait("vector", "mm", drain=True)
                    bp.wait("vector", "st")
                    bp.emit("vector", lambda e: e.tensor_copy(out=self.qtb[:, :], in_=self.pT[:, 0:128]), inc="ev")
                    bp.wait("sync", "ev")
                    boff = ((sp * 16 + qc) * 8 + tgg) * 128
                    bp.emit("sync", lambda e, boff=boff: e.dma_start(
                        out=self.qkt_sp[boff:boff + 128, :], in_=self.qtb[:, :]), inc="st", n=16)
        elif g < 10:
            bp.wait("vector", "mm", drain=True)
            self._rope_kappa(ps, tgg, self.qtmp[:, 0:256], self.qtmp[:, 256:1024])
            kk1 = self.arD[:, 0:4096]
            kk2 = self.arD[:, 4096:8192]
            c0 = 512 * tgg + 256 * (g - 8)
            bp.emit("vector", lambda e, c0=c0, kk1=kk1: e.tensor_copy(
                out=kk1[:, c0:c0 + 256], in_=self.qtmp[:, 0:256]), inc="ev")
            bp.emit("vector", lambda e, c0=c0, kk1=kk1, kk2=kk2: e.tensor_sub(
                out=kk2[:, c0:c0 + 256], in0=self.qtmp[:, 0:256], in1=kk1[:, c0:c0 + 256]), inc="ev")
        else:
            bp.wait("vector", "mm", drain=True)
            bp.wait("vector", "st")
            bp.emit("vector", lambda e, ps=ps: e.tensor_copy(out=self.vt1[:, 0:256], in_=ps[:, 0:256]), inc="ev")
            bp.emit("vector", lambda e, ps=ps: e.tensor_sub(out=self.vt2[:, 0:256], in0=ps[:, 0:256],
                                                            in1=self.vt1[:, 0:256]), inc="ev")
            bp.wait("sync", "ev")
            c0 = 256 * (g - 10)
            r0 = 128 * tgg
            bp.emit("sync", lambda e, r0=r0, c0=c0: e.dma_start(
                out=self.v_sp[r0:r0 + 128, c0:c0 + 256], in_=self.vt1[:, 0:256]), inc="st", n=16)
            bp.emit("sync", lambda e, r0=r0, c0=c0: e.dma_start(
                out=self.v_sp[1024 + r0:1024 + r0 + 128, c0:c0 + 256], in_=self.vt2[:, 0:256]), inc="st", n=16)

    def _rope_kappa(self, ps, tgg, dst, tmp):
        bp = self.bp
        nh = 4
        cosap = self.ropc[:, 64 * tgg: 64 * tgg + 64]
        sinap = self.rops[:, 64 * tgg: 64 * tgg + 64]
        h3 = lambda ap: ap.rearrange("p (h d) -> p h d", h=nh)
        a = tmp[:, 0:256]
        b = tmp[:, 256:512]
        bp.emit("vector", lambda e: e.tensor_mul(
            out=h3(a), in0=h3(ps[:, 0:256]), in1=cosap[:, None, :].broadcast_to((128, nh, 64))))
        bp.emit("vector", lambda e: e.tensor_mul(
            out=h3(b), in0=h3(ps[:, 0:256]), in1=sinap[:, None, :].broadcast_to((128, nh, 64))), inc="ev")
        a4 = a.rearrange("p (h q d) -> p h q d", h=nh, q=2)
        b4 = b.rearrange("p (h q d) -> p h q d", h=nh, q=2)
        o4 = dst.rearrange("p (h q d) -> p h q d", h=nh, q=2)
        bp.emit("vector", lambda e: e.tensor_sub(out=o4[:, :, 0, :], in0=a4[:, :, 0, :], in1=b4[:, :, 1, :]))
        bp.emit("vector", lambda e: e.tensor_add(out=o4[:, :, 1, :], in0=a4[:, :, 1, :], in1=b4[:, :, 0, :]), inc="ev")
        mn = tmp[:, 0:256]
        ex = tmp[:, 256:512]
        bp.emit("vector", lambda e: e.tensor_scalar_min(out=mn, in0=dst, scalar1=0), inc="ev")
        bp.wait("scalar", "ev")
        bp.emit("scalar", lambda e: e.activation(out=ex, in_=mn, func=ACT.Exp), inc="sc")
        bp.emit("vector", lambda e: e.tensor_scalar_max(out=dst, in0=dst, scalar1=0))
        bp.wait("vector", "sc")
        bp.emit("vector", lambda e: e.tensor_add(out=dst, in0=dst, in1=ex), inc="ev")

    def p1b(self, upto=99):
        bp = self.bp
        R_QG, R_TMP = self.R_QG, self.R_TMP
        qgb = self.arC[:, 0:2048]
        work = self.arC[:, 2048:3072]
        ksum = self.arC[:, 3072:3328]
        kk1 = self.arD[:, 0:4096]
        kk2 = self.arD[:, 4096:8192]
        a1 = self.arB[:, 0:2048]
        a2 = self.arB[:, 4096:6144]
        outs1 = self.arB[:, 8192:10240]
        outs2 = self.arB[:, 12288:14336]

        bp.wait("sync", "ev")
        bp.emit("sync", lambda e: e.dma_start(out=self.ar_qg_i[0:1, :],
                                              in_=self.rows[0:1, R_QG:R_QG + 2048]), inc="st", n=16)
        bp.wait("gpsimd", "st")
        bp.emit("gpsimd", lambda e: e.collective_compute(
            "AllReduce", ALU.add, replica_groups=PAIRS,
            ins=[self.ar_qg_i[:, :]], outs=[self.ar_qg_o[:, :]]), inc="cc")
        bp.wait("sync", "cc")
        bp.emit("sync", lambda e: e.dma_start(out=self.rows[0:1, R_QG:R_QG + 2048],
                                              in_=self.ar_qg_o[0:1, :]), inc="ld", n=16)
        bp.wait("vector", "ld")
        bp.emit("vector", lambda e: e.tensor_single_scalar(
            out=self.rows[0:1, R_QG:R_QG + 2048], in_=self.rows[0:1, R_QG:R_QG + 2048],
            scalar=self.CO0(self.C_I2048), op=ALU.mult), inc="ev")
        for i in range(4):
            self.brow(self.rows[0:1, R_QG + 512 * i:R_QG + 512 * i + 512], 512,
                      qgb[:, 512 * i:512 * i + 512])

        for tg in range(8):
            for hf in range(2):
                ks = ksum
                bp.emit("vector", lambda e, tg=tg, hf=hf, ks=ks: e.tensor_add(
                    out=ks, in0=kk1[:, 512 * tg + 256 * hf:512 * tg + 256 * hf + 256],
                    in1=kk2[:, 512 * tg + 256 * hf:512 * tg + 256 * hf + 256]))
                for grp in range(4):
                    s3 = ks[:, 64 * grp:64 * grp + 64][:, None, :].broadcast_to((128, 4, 64))
                    qg3 = qgb[:, 1024 * hf + 256 * grp:1024 * hf + 256 * grp + 256] \
                        .rearrange("p (r d) -> p r d", r=4)
                    w3 = work[:, 256 * grp:256 * grp + 256].rearrange("p (r d) -> p r d", r=4)
                    bp.emit("vector", lambda e, s3=s3, qg3=qg3, w3=w3: e.scalar_tensor_tensor(
                        out=w3, in0=s3, scalar=-1.0, in1=qg3, op0=ALU.add, op1=ALU.mult))
                bp.emit("vector", lambda e, tg=tg, hf=hf: e.reduce_sum(
                    out=self.prowb[:, 32 * tg + 16 * hf:32 * tg + 16 * hf + 16],
                    in_=work.rearrange("p (h d) -> p h d", h=16), axis=mybir.AxisListType.X), inc="ev")

        if upto <= 1:
            return
        if upto <= 2:
            return
        mx = self.qtmp[:, 512:544]
        bp.wait("scalar", "ev")
        bp.emit("scalar", lambda e: e.activation(
            out=self.prowb[:, :], in_=self.prowb[:, :], func=ACT.Exp), inc="sc")
        bp.wait("tensor", "sc")
        def mk_zmm(e):
            last = None
            for tg in range(8):
                last = e.matmul(self.pR[0:1, 0:32], self.onec[:, :],
                                self.prowb[:, 32 * tg:32 * tg + 32],
                                start=(tg == 0), stop=(tg == 7))
            return last
        bp.emit("tensor", mk_zmm, inc="mm")
        bp.wait("vector", "mm", drain=True)
        bp.emit("vector", lambda e: e.tensor_copy(out=self.rows[0:1, R_TMP + 32:R_TMP + 64],
                                                  in_=self.pR[0:1, 0:32]), inc="ev")

        banks = (self.pC, self.pD, self.pE, self.pF)
        bp.wait("vector", "mm")
        for p in banks:
            bp.emit("vector", lambda e, p=p: e.memset(p[:, :], 0.0))
        bp.emit("vector", lambda e: e.nop(), inc="ev")
        ev_zero = bp.val("ev")
        bp.wait("tensor", "ev", ev_zero)
        for tg in range(8):
            for hf in range(2):
                bp.wait("vector", "mm")
                bp.emit("vector", lambda e, tg=tg, hf=hf: e.tensor_add(
                    out=ksum, in0=kk1[:, 512 * tg + 256 * hf:512 * tg + 256 * hf + 256],
                    in1=kk2[:, 512 * tg + 256 * hf:512 * tg + 256 * hf + 256]))
                p3 = self.prowb[:, 32 * tg + 16 * hf:32 * tg + 16 * hf + 16] \
                    .rearrange("p (g r) -> p g r", g=4)[:, :, :, None].broadcast_to((128, 4, 4, 64))
                s4 = ksum.rearrange("p (g d) -> p g d", g=4)[:, :, None, :] \
                    .broadcast_to((128, 4, 4, 64))
                bp.emit("vector", lambda e, p3=p3, s4=s4: e.tensor_mul(
                    out=work.rearrange("p (g r d) -> p g r d", g=4, r=4), in0=p3, in1=s4))
                d1 = a1[:, 1024 * hf:1024 * hf + 1024]
                d2 = a2[:, 1024 * hf:1024 * hf + 1024]
                bp.emit("vector", lambda e, d1=d1: e.tensor_copy(out=d1, in_=work), inc="ev")
                bp.emit("vector", lambda e, d1=d1, d2=d2: e.tensor_sub(out=d2, in0=work, in1=d1), inc="ev")
            bp.wait("sync", "mm")
            bp.emit("sync", lambda e, tg=tg: e.dma_start(
                out=self.vt1[:, :], in_=self.v_sp[128 * tg:128 * tg + 128, :]), inc="ld", n=16)
            bp.emit("sync", lambda e, tg=tg: e.dma_start(
                out=self.vt2[:, :], in_=self.v_sp[1024 + 128 * tg:1024 + 128 * tg + 128, :]), inc="ld", n=16)
            bp.wait("tensor", "ev")
            bp.wait("tensor", "ld")
            def mk_outer(e):
                last = None
                for h in range(32):
                    bk = banks[h // 8]
                    c0 = 64 * (h % 8)
                    kv = h // 4
                    va1 = a1[:, 64 * h:64 * h + 64]
                    va2 = a2[:, 64 * h:64 * h + 64]
                    vv1 = self.vt1[:, 64 * kv:64 * kv + 64]
                    vv2 = self.vt2[:, 64 * kv:64 * kv + 64]
                    e.matmul(bk[0:64, c0:c0 + 64], va1, vv1, start=False, stop=False,
                             skip_group_check=True)
                    e.matmul(bk[0:64, c0:c0 + 64], va2, vv1, start=False, stop=False,
                             skip_group_check=True)
                    last = e.matmul(bk[0:64, c0:c0 + 64], va1, vv2, start=False, stop=False,
                                    skip_group_check=True)
                return last
            bp.emit("tensor", mk_outer, inc="mm")

        bp.wait("vector", "mm", drain=True)
        osb = self.arC[0:64, 0:2048]
        for i, p in enumerate(banks):
            bp.emit("vector", lambda e, i=i, p=p: e.tensor_copy(
                out=osb[:, 512 * i:512 * i + 512], in_=p[0:64, :]), inc="ev")
        if upto <= 3:
            return
        bp.emit("vector", lambda e: e.memset(self.qtmp[0:64, 0:32], 0.0), inc="ev")
        bp.wait("sync", "ev")
        bp.emit("sync", lambda e: e.dma_start(out=self.ar_oz_i[0:64, 2048:2080],
                                              in_=self.qtmp[0:64, 0:32]), inc="st", n=16)
        bp.emit("sync", lambda e: e.dma_start(out=self.ar_oz_i[0:64, 0:2048], in_=osb), inc="st", n=16)
        bp.emit("sync", lambda e: e.dma_start(out=self.ar_oz_i[0:1, 2048:2080],
                                              in_=self.rows[0:1, R_TMP + 32:R_TMP + 64]), inc="st", n=16)
        bp.wait("gpsimd", "st")
        bp.emit("gpsimd", lambda e: e.collective_compute(
            "AllReduce", ALU.add, replica_groups=PAIRS,
            ins=[self.ar_oz_i[:, :]], outs=[self.ar_oz_o[:, :]]), inc="cc")
        bp.wait("sync", "cc")
        bp.emit("sync", lambda e: e.dma_start(out=osb, in_=self.ar_oz_o[0:64, 0:2048]), inc="ld", n=16)
        bp.emit("sync", lambda e: e.dma_start(out=self.rows[0:1, R_TMP + 32:R_TMP + 64],
                                              in_=self.ar_oz_o[0:1, 2048:2080]), inc="ld", n=16)
        bp.wait("vector", "ld")
        bp.emit("vector", lambda e: e.reciprocal(out=self.rows[0:1, R_TMP + 64:R_TMP + 96],
                                                 in_=self.rows[0:1, R_TMP + 32:R_TMP + 64]))
        self.vbubble()
        bp.emit("vector", lambda e: e.tensor_single_scalar(
            out=self.rows[0:1, R_TMP + 64:R_TMP + 96], in_=self.rows[0:1, R_TMP + 64:R_TMP + 96],
            scalar=self.CO0(self.C_SEQ), op=ALU.mult), inc="ev")
        self.vbubble()
        self.brow(self.rows[0:1, R_TMP + 64:R_TMP + 96], 32, mx, bubble=True)
        zs3 = mx[0:64, :, None].broadcast_to((64, 32, 64))
        o3 = osb.rearrange("p (h d) -> p h d", h=32)
        bp.emit("vector", lambda e: e.tensor_mul(out=o3, in0=o3, in1=zs3), inc="ev")
        bp.emit("vector", lambda e: e.tensor_copy(out=outs1[0:64, :], in_=osb), inc="ev")
        bp.emit("vector", lambda e: e.tensor_sub(out=outs2[0:64, :], in0=osb, in1=outs1[0:64, :]), inc="ev")
        self.outs1, self.outs2 = outs1, outs2

    def p1c(self):
        bp = self.bp
        outs1, outs2 = self.outs1, self.outs2
        xphic = self.arC[:, 2048:2304]
        ctxf = self.arC[:, 2304:2560]
        wstage = self.arC[:, 2560:3584]
        ctx1 = self.arB[:, 0:256]
        ctx2 = self.arB[:, 256:512]
        qh = self.arB[:, 4096:5120]
        for tN in range(4):
            for ck in range(16):
                bp.wait("sync", "ev")
                bp.emit("sync", lambda e, ck=ck, tN=tN: e.dma_start(
                    out=xphic, in_=self.xphi_sp[128 * ck:128 * ck + 128, 256 * tN:256 * tN + 256]),
                    inc="ld", n=16)
                for hh in range(2):
                    h = 2 * ck + hh
                    for sp in range(2):
                        dst = qh[0:64, 256 * (2 * hh + sp):256 * (2 * hh + sp) + 256]
                        for tc2 in range(2):
                            boff = ((sp * 16 + ck) * 8 + 2 * tN + tc2) * 128
                            roff = 64 * hh
                            bp.emit("sync", lambda e, dst=dst, boff=boff, roff=roff, tc2=tc2: e.dma_start(
                                out=dst[:, 128 * tc2:128 * tc2 + 128],
                                in_=self.qkt_sp[boff + roff:boff + roff + 64, :]), inc="ld", n=16)
                bp.wait("tensor", "ld")
                bp.wait("tensor", "ev")
                def mk_res(e, ck=ck):
                    last = None
                    for hh in range(2):
                        h = 2 * ck + hh
                        ps = self.pE if hh == 0 else self.pF
                        o1 = outs1[0:64, 64 * h:64 * h + 64]
                        o2 = outs2[0:64, 64 * h:64 * h + 64]
                        q1 = qh[0:64, 256 * (2 * hh):256 * (2 * hh) + 256]
                        q2 = qh[0:64, 256 * (2 * hh + 1):256 * (2 * hh + 1) + 256]
                        e.matmul(ps[0:64, 0:256], o1, q1, start=True, stop=False)
                        e.matmul(ps[0:64, 0:256], o2, q1, start=False, stop=False)
                        last = e.matmul(ps[0:64, 0:256], o1, q2, start=False, stop=True)
                    return last
                bp.emit("tensor", mk_res, inc="mm")
                bp.wait("vector", "mm", drain=True)
                bp.emit("vector", lambda e: e.tensor_mul(
                    out=ctxf[0:64, :], in0=xphic[0:64, :], in1=self.pE[0:64, 0:256]))
                bp.emit("vector", lambda e: e.tensor_mul(
                    out=ctxf[64:128, :], in0=xphic[64:128, :], in1=self.pF[0:64, 0:256]), inc="ev")
                bp.emit("vector", lambda e: e.tensor_copy(out=ctx1, in_=ctxf), inc="ev")
                bp.emit("vector", lambda e: e.tensor_sub(out=ctx2, in0=ctxf, in1=ctx1), inc="ev")
                bp.wait("sync", "ev")
                bp.emit("sync", lambda e, ck=ck, tN=tN: e.dma_start(
                    out=self.ctx_sp[128 * ck:128 * ck + 128, 256 * tN:256 * tN + 256],
                    in_=ctx1), inc="st", n=16)
                bp.emit("sync", lambda e, ck=ck, tN=tN: e.dma_start(
                    out=self.ctx_sp[2048 + 128 * ck:2048 + 128 * ck + 128, 256 * tN:256 * tN + 256],
                    in_=ctx2), inc="st", n=16)

        wo1 = self.arB[:, 8192:10240]
        wo2 = self.arB[:, 12288:14336]
        ctxa = self.arB[:, 0:2048]
        ctxb = self.arB[:, 4096:6144]
        wov = self.wo.rearrange("p (k c) -> p k c", k=NCH)
        fop = [(self.pA, 0), (self.pA, 256), (self.pB, 0), (self.pB, 256),
               (self.pC, 0), (self.pC, 256), (self.pD, 0), (self.pD, 256)]
        for foh in range(2):
            for tN in range(4):
                bp.wait("vector", "mm")
                for p in (self.pA, self.pB, self.pC, self.pD):
                    bp.emit("vector", lambda e, p=p: e.memset(p[:, :], 0.0))
                bp.emit("vector", lambda e: e.nop(), inc="ev")
                ev_zero = bp.val("ev")
                bp.wait("tensor", "ev", ev_zero)
                for ck in range(16):
                    bp.wait("sync", "mm")
                    cslot1 = ctxa[:, 256 * (ck % 4):256 * (ck % 4) + 256]
                    cslot2 = ctxb[:, 256 * (ck % 4):256 * (ck % 4) + 256]
                    bp.emit("sync", lambda e, ck=ck, tN=tN, cslot1=cslot1: e.dma_start(
                        out=cslot1, in_=self.ctx_sp[128 * ck:128 * ck + 128, 256 * tN:256 * tN + 256]),
                        inc="ld", n=16)
                    bp.emit("sync", lambda e, ck=ck, tN=tN, cslot2=cslot2: e.dma_start(
                        out=cslot2, in_=self.ctx_sp[2048 + 128 * ck:2048 + 128 * ck + 128,
                                                    256 * tN:256 * tN + 256]),
                        inc="ld", n=16)
                    bp.wait("sync", "ev")
                    bp.emit("sync", lambda e, ck=ck, foh=foh: e.dma_start(
                        out=wstage, in_=wov[:, ck, 1024 * foh:1024 * foh + 1024]), inc="ld", n=16)
                    bp.wait("vector", "ld")
                    bp.wait("vector", "mm")
                    w1c = wo1[:, 0:1024]
                    w2c = wo2[:, 0:1024]
                    bp.emit("vector", lambda e, w1c=w1c: e.tensor_copy(out=w1c, in_=wstage), inc="ev")
                    bp.emit("vector", lambda e, w1c=w1c, w2c=w2c: e.tensor_sub(
                        out=w2c, in0=wstage, in1=w1c), inc="ev")
                    bp.wait("tensor", "ev")
                    bp.wait("tensor", "ld")
                    def mk_wo(e, ck=ck, cslot1=cslot1, cslot2=cslot2, w1c=w1c, w2c=w2c):
                        last = None
                        for fo in range(8):
                            bk, c0 = fop[fo]
                            ww1 = w1c[:, 128 * fo:128 * fo + 128]
                            ww2 = w2c[:, 128 * fo:128 * fo + 128]
                            e.matmul(bk[:, c0:c0 + 256], ww1, cslot1, start=False, stop=False,
                                     skip_group_check=True)
                            e.matmul(bk[:, c0:c0 + 256], ww2, cslot1, start=False, stop=False,
                                     skip_group_check=True)
                            last = e.matmul(bk[:, c0:c0 + 256], ww1, cslot2, start=False, stop=False,
                                            skip_group_check=True)
                        return last
                    bp.emit("tensor", mk_wo, inc="mm")
                bp.wait("vector", "mm", drain=True)
                for fo in range(8):
                    bk, c0 = fop[fo]
                    fog = 8 * foh + fo
                    xs = self.xres[:, fog * T + 256 * tN: fog * T + 256 * tN + 256]
                    bp.emit("vector", lambda e, bk=bk, c0=c0, xs=xs: e.tensor_add(
                        out=xs, in0=xs, in1=bk[:, c0:c0 + 256]), inc="ev")

    def p2(self):
        bp = self.bp
        wstg = [self.arC[:, 0:2048], self.arC[:, 2048:4096]]
        in2v = self.in2.rearrange("p (k c) -> p k c", k=NCH)
        q8 = self.ar8[:, 0:4096]
        d2 = [self.ar8[:, 4096:6144], self.ar8[:, 6144:8192]]
        for tq in range(4):
            t0 = 256 * tq
            self.ln_stats(t0, 256, rmul_ap=self.dqb[0:1, 1:2])
            bp.wait("vector", "mm")
            for k in range(NCH):
                self.ln_apply(k, t0, 256, q8[:, 256 * k:256 * k + 256], self.scr[:, 0:256])
            ev_q8 = bp.val("ev")
            mm_fo = [0] * NCH
            for fo in range(NCH):
                pp = fo % 2
                if fo >= 2:
                    bp.wait("sync", "ev")
                bp.emit("sync", lambda e, fo=fo, pp=pp: e.dma_start(
                    out=wstg[pp].rearrange("p (k c) -> p k c", k=NCH),
                    in_=in2v[:, :, 128 * fo:128 * fo + 128]), inc="ld", n=16)
                bp.wait("vector", "ld")
                if fo >= 2:
                    bp.wait("vector", "mm", mm_fo[fo - 2])
                bp.emit("vector", lambda e, pp=pp: e.tensor_scalar_mul(
                    out=d2[pp], in0=wstg[pp], scalar1=self.dqb[:, 2:3]), inc="ev")
                bp.wait("tensor", "ev")
                if fo == 0:
                    bp.wait("tensor", "ev", ev_q8)
                ps = self.pA if fo % 2 == 0 else self.pB
                def mk_ffn(e, pp=pp, ps=ps):
                    last = None
                    for k in range(NCH):
                        last = e.matmul(ps[:, 0:256], d2[pp][:, 128 * k:128 * k + 128],
                                        q8[:, 256 * k:256 * k + 256],
                                        start=(k == 0), stop=(k == NCH - 1))
                    return last
                bp.emit("tensor", mk_ffn, inc="mm")
                mm_fo[fo] = bp.val("mm")
                bp.wait("vector", "mm", drain=True)
                xs = self.xres[:, fo * T + t0: fo * T + t0 + 256]
                bp.emit("vector", lambda e, xs=xs, ps=ps: e.scalar_tensor_tensor(
                    out=xs, in0=ps[:, 0:256], scalar=self.dqb[:, 0:1], in1=xs,
                    op0=ALU.mult, op1=ALU.add), inc="ev")

    def p3a(self):
        bp = self.bp
        x1v = self.arA[:, 0:8192]
        x2v = self.arA[:, 8192:16384]
        wsp = [(self.arB[:, 0:4096], self.arB[:, 4096:8192]),
               (self.arB[:, 8192:12288], self.arB[:, 12288:16384])]
        wstg = [self.arC[:, 0:2048], self.arC[:, 2048:4096]]
        wqkvv = self.wqkv.rearrange("p (k c) -> p k c", k=NCH)

        for th in range(2):
            t0 = 512 * th
            self.ln_stats(t0, 512)
            bp.wait("vector", "mm")
            for k in range(NCH):
                self.ln_apply(k, t0, 512, self.scr[:, 0:512], self.scr[:, 0:512])
                x1k = x1v[:, 512 * k:512 * k + 512]
                x2k = x2v[:, 512 * k:512 * k + 512]
                bp.emit("vector", lambda e, x1k=x1k: e.tensor_copy(out=x1k, in_=self.scr[:, 0:512]), inc="ev")
                bp.emit("vector", lambda e, x1k=x1k, x2k=x2k: e.tensor_sub(
                    out=x2k, in0=self.scr[:, 0:512], in1=x1k), inc="ev")
            ev_xpair = bp.val("ev")

            mm_oc = [0] * 16
            for oc in range(16):
                pp = oc % 2
                for kh in range(2):
                    sb = wstg[(2 * oc + kh) % 2]
                    if 2 * oc + kh >= 2:
                        bp.wait("sync", "ev")
                    bp.emit("sync", lambda e, oc=oc, kh=kh, sb=sb: e.dma_start(
                        out=sb[:, 0:1024].rearrange("p (k c) -> p k c", k=8),
                        in_=wqkvv[:, 8 * kh:8 * kh + 8, 128 * oc:128 * oc + 128]),
                        inc="ld", n=16)
                    bp.wait("vector", "ld")
                    if oc >= 2 and kh == 0:
                        bp.wait("vector", "mm", mm_oc[oc - 2])
                    d1 = wsp[pp][0][:, 1024 * kh:1024 * kh + 1024]
                    d2 = wsp[pp][1][:, 1024 * kh:1024 * kh + 1024]
                    bp.emit("vector", lambda e, sb=sb, d1=d1: e.tensor_copy(out=d1, in_=sb[:, 0:1024]), inc="ev")
                    bp.emit("vector", lambda e, sb=sb, d1=d1, d2=d2: e.tensor_sub(
                        out=d2, in0=sb[:, 0:1024], in1=d1), inc="ev")
                ev_spl = bp.val("ev")
                bp.wait("tensor", "ev", ev_spl)
                if oc == 0:
                    bp.wait("tensor", "ev", ev_xpair)
                def mk_q(e, pp=pp):
                    last = None
                    for k in range(NCH):
                        w1 = wsp[pp][0][:, 128 * k:128 * k + 128]
                        w2 = wsp[pp][1][:, 128 * k:128 * k + 128]
                        xs1 = x1v[:, 512 * k:512 * k + 512]
                        xs2 = x2v[:, 512 * k:512 * k + 512]
                        e.matmul(self.pE[:, 0:512], w1, xs1, start=(k == 0), stop=False)
                        e.matmul(self.pE[:, 0:512], w2, xs1, start=False, stop=False)
                        last = e.matmul(self.pE[:, 0:512], w1, xs2, start=False, stop=(k == NCH - 1))
                    return last
                bp.wait("tensor", "st")
                bp.emit("tensor", mk_q, inc="mm")
                mm_oc[oc] = bp.val("mm")
                bp.wait("vector", "mm", drain=True)
                bp.wait("vector", "st")
                bp.emit("vector", lambda e: e.tensor_single_scalar(
                    out=self.qtmp[:, 0:512], in_=self.pE[:, 0:512], scalar=self.CO(self.C_ISQ),
                    op=ALU.mult), inc="ev")
                bp.emit("vector", lambda e: e.tensor_copy(out=self.vt1[:, :], in_=self.qtmp[:, 0:512]), inc="ev")
                bp.emit("vector", lambda e: e.tensor_sub(out=self.vt2[:, :], in0=self.qtmp[:, 0:512],
                                                         in1=self.vt1[:, :]), inc="ev")
                bp.wait("sync", "ev")
                for sp, tile in ((0, self.vt1), (1, self.vt2)):
                    bp.emit("sync", lambda e, sp=sp, tile=tile, oc=oc, t0=t0: e.dma_start(
                        out=self.q_sp[2048 * sp + 128 * oc:2048 * sp + 128 * oc + 128, t0:t0 + 512],
                        in_=tile[:, :]), inc="st", n=16)

            mm_g = [0] * 16
            for g in range(16):
                pp = g % 2
                c0 = 2048 + 256 * g
                for kh in range(2):
                    sb = wstg[(2 * g + kh) % 2]
                    bp.wait("sync", "ev")
                    bp.emit("sync", lambda e, c0=c0, kh=kh, sb=sb: e.dma_start(
                        out=sb.rearrange("p (k c) -> p k c", k=8),
                        in_=wqkvv[:, 8 * kh:8 * kh + 8, c0:c0 + 256]), inc="ld", n=16)
                    bp.wait("vector", "ld")
                    if g >= 2 and kh == 0:
                        bp.wait("vector", "mm", mm_g[g - 2])
                    d1 = wsp[pp][0][:, 2048 * kh:2048 * kh + 2048]
                    d2 = wsp[pp][1][:, 2048 * kh:2048 * kh + 2048]
                    bp.emit("vector", lambda e, sb=sb, d1=d1: e.tensor_copy(out=d1, in_=sb), inc="ev")
                    bp.emit("vector", lambda e, sb=sb, d1=d1, d2=d2: e.tensor_sub(out=d2, in0=sb, in1=d1), inc="ev")
                bp.wait("tensor", "ev", bp.val("ev"))
                for tg in range(4):
                    tgg = 4 * th + tg
                    ps = self.pA if tg % 2 == 0 else self.pB
                    bp.wait("tensor", "ev")
                    def mk_kv(e, pp=pp, tg=tg, ps=ps):
                        last = None
                        for k in range(NCH):
                            xs1 = x1v[:, 512 * k + 128 * tg: 512 * k + 128 * tg + 128]
                            xs2 = x2v[:, 512 * k + 128 * tg: 512 * k + 128 * tg + 128]
                            ws1 = wsp[pp][0][:, 256 * k:256 * k + 256]
                            ws2 = wsp[pp][1][:, 256 * k:256 * k + 256]
                            e.matmul(ps[:, 0:256], xs1, ws1, start=(k == 0), stop=False)
                            e.matmul(ps[:, 0:256], xs2, ws1, start=False, stop=False)
                            last = e.matmul(ps[:, 0:256], xs1, ws2, start=False, stop=(k == NCH - 1))
                        return last
                    bp.emit("tensor", mk_kv, inc="mm")
                    mm_g[g] = bp.val("mm")
                    bp.wait("vector", "mm", drain=True)
                    bp.wait("vector", "st")
                    bp.emit("vector", lambda e, ps=ps: e.tensor_copy(out=self.vt1[:, 0:256], in_=ps[:, 0:256]), inc="ev")
                    bp.emit("vector", lambda e, ps=ps: e.tensor_sub(out=self.vt2[:, 0:256], in0=ps[:, 0:256],
                                                                    in1=self.vt1[:, 0:256]), inc="ev")
                    bp.wait("sync", "ev")
                    qq = tgg // 2
                    ro = 128 * (tgg % 2)
                    kv = 0 if g < 8 else 1
                    cc = 256 * (g % 8)
                    for sp, tile in ((0, self.vt1), (1, self.vt2)):
                        region = 2 * kv + sp
                        tbase = 256 * region + ro
                        bp.emit("sync", lambda e, qq=qq, tbase=tbase, cc=cc, tile=tile: e.dma_start(
                            out=self.kv_in[qq].rearrange("(t e) c -> t e c", e=8)
                                [tbase:tbase + 128, cc // 256, :],
                            in_=tile[:, 0:256]), inc="st", n=16)

        for qq in range(4):
            bp.wait("gpsimd", "st")
            bp.emit("gpsimd", lambda e, qq=qq: e.collective_compute(
                "AllGather", ALU.bypass, replica_groups=PAIRS,
                ins=[self.kv_in[qq][:, :]], outs=[self.kv_out[qq][:, :]]), inc="cc")

    def p3b(self):
        bp = self.bp
        R_QG = self.R_QG
        kvt = [self.arA[:, 0:8192], self.arA[:, 8192:16384]]
        qt = self.arB[:, 0:4096]
        kts = [self.arB[:, 4096:4608], self.arB[:, 4608:5120]]
        a1 = [self.arB[:, 8192:8704], self.arB[:, 8704:9216]]
        a2 = [self.arB[:, 9216:9728], self.arB[:, 9728:10240]]
        cs1 = self.arB[:, 12288:12800]
        cs2 = self.arB[:, 12800:13312]
        Eacc = [self.arC[:, 512 * i:512 * i + 512] for i in range(4)]
        Ech = [self.arC[:, 2048:2560], self.arC[:, 2560:3072]]
        Zb = self.arC[:, 3072:3584]
        cfs = self.arC[:, 3584:4096]
        combp = {(0, 0): self.pC, (0, 1): self.pD, (1, 0): self.pE, (1, 1): self.pF}
        psp = {(0, 0): (0, self.pA), (0, 1): (1, self.pB),
               (1, 0): (0, self.pA), (1, 1): (1, self.pB)}

        def kv_load(kvc, dma_waits_mm):
            kvt0 = 128 * kvc
            r, qq, ro = kvt0 // 1024, (kvt0 % 1024) // 256, kvt0 % 256
            pp = kvc % 2
            bp.wait("sync", "mm", dma_waits_mm)
            for reg in range(4):
                trow = 1024 * r + 256 * reg + ro
                dst = kvt[pp][:, 2048 * reg:2048 * reg + 2048]
                bp.emit("sync", lambda e, qq=qq, trow=trow, dst=dst: e.dma_start(
                    out=dst.rearrange("p (e c) -> p e c", e=8),
                    in_=self.kv_out[qq].rearrange("(t e) c -> t e c", e=8)[trow:trow + 128, :, :]),
                    inc="ld", n=16)
            return bp.val("ld")

        def tr_piece(kvc, i, ld_val, ev_prev):
            h2, sp = i // 2, i % 2
            hh = 2 * self._grp + h2
            pp = kvc % 2
            ksrc = kvt[pp][:, 2048 * sp + 128 * hh:2048 * sp + 128 * hh + 128]
            bp.wait("tensor", "ld", ld_val)
            bp.wait("tensor", "ev", ev_prev)
            bp.emit("tensor", lambda e, ksrc=ksrc: e.transpose(
                self.pT[:, 0:128], ksrc, self.idn[:, :]), inc="mm")
            bp.wait("vector", "mm", drain=True)
            dst = kts[pp][:, 256 * h2 + 128 * sp:256 * h2 + 128 * sp + 128]
            bp.emit("vector", lambda e, dst=dst: e.tensor_copy(out=dst, in_=self.pT[:, 0:128]), inc="ev")
            return bp.val("ev")

        for grp in range(8):
            self._grp = grp
            bp.wait("sync", "mm")
            if grp == 0:
                bp.wait("sync", "st")
                bp.wait("sync", "cc")
            for h2 in range(2):
                for sp in range(2):
                    dst = qt[:, 1024 * (2 * h2 + sp):1024 * (2 * h2 + sp) + 1024]
                    r0 = 2048 * sp + 128 * (2 * grp + h2)
                    bp.emit("sync", lambda e, dst=dst, r0=r0: e.dma_start(
                        out=dst, in_=self.q_sp[r0:r0 + 128, :]), inc="ld", n=16)
            q_ld = bp.val("ld")
            bp.wait("vector", "mm")
            for i in range(4):
                bp.emit("vector", lambda e, i=i: e.memset(Eacc[i], 0.0))
            bp.emit("vector", lambda e: e.nop(), inc="ev")
            ev_eacc0 = bp.val("ev")

            ld0 = kv_load(0, 0 if grp == 0 else self._av_mm)
            ev_tr = bp.val("ev")
            for i in range(4):
                ev_tr = tr_piece(0, i, ld0, ev_tr)
            ev_kts = [ev_tr, 0]
            ld_kv = [ld0, 0]
            sc_free = [0, 0]
            av_done = {}
            ev_cons = {}
            rr = 0
            for kvc in range(16):
                pp = kvc % 2
                if kvc + 1 < 16:
                    ld_kv[1 - pp] = kv_load(kvc + 1, self._av_mm if kvc >= 1 else (0 if grp == 0 else self._av_mm))
                for idx, (h2, qtt) in enumerate(((0, 0), (0, 1), (1, 0), (1, 1))):
                    if kvc + 1 < 16:
                        ev_tr = tr_piece(kvc + 1, idx, ld_kv[1 - pp], ev_tr)
                        if idx == 3:
                            ev_kts[1 - pp] = ev_tr
                    psi, ps = psp[(h2, qtt)]
                    cb = combp[(h2, qtt)]
                    pp2 = rr % 2
                    rr += 1
                    q1 = qt[:, 1024 * (2 * h2) + 512 * qtt:1024 * (2 * h2) + 512 * qtt + 512]
                    q2 = qt[:, 1024 * (2 * h2 + 1) + 512 * qtt:1024 * (2 * h2 + 1) + 512 * qtt + 512]
                    kT1 = kts[pp][:, 256 * h2:256 * h2 + 128]
                    kT2 = kts[pp][:, 256 * h2 + 128:256 * h2 + 256]
                    bp.wait("tensor", "ev", ev_kts[pp])
                    if kvc == 0 and idx == 0:
                        bp.wait("tensor", "ld", q_ld)
                        bp.wait("tensor", "ev", ev_eacc0)
                    bp.wait("tensor", "sc", sc_free[psi])
                    def mk_s(e, ps=ps, kT1=kT1, kT2=kT2, q1=q1, q2=q2):
                        e.matmul(ps[:, 0:512], kT1, q1, start=True, stop=False)
                        e.matmul(ps[:, 0:512], kT1, q2, start=False, stop=False)
                        return e.matmul(ps[:, 0:512], kT2, q1, start=False, stop=True)
                    bp.emit("tensor", mk_s, inc="mm")
                    bp.wait("scalar", "mm", drain=True)
                    if pp2 in ev_cons:
                        bp.wait("scalar", "ev", ev_cons[pp2])
                    if pp2 in av_done:
                        bp.wait("scalar", "mm", av_done[pp2])
                    bp.emit("scalar", lambda e, ps=ps, pp2=pp2: e.activation(
                        out=a1[pp2][:, :], in_=ps[:, 0:512], func=ACT.Exp), inc="sc")
                    bp.emit("scalar", lambda e, ps=ps, pp2=pp2: e.activation(
                        out=Ech[pp2][:, :], in_=ps[:, 0:512], func=ACT.Exp), inc="sc")
                    sc_free[psi] = bp.val("sc")
                    bp.wait("vector", "sc")
                    if pp2 in av_done:
                        bp.wait("vector", "mm", av_done[pp2])
                    bp.emit("vector", lambda e, pp2=pp2: e.tensor_sub(
                        out=a2[pp2][:, :], in0=Ech[pp2][:, :], in1=a1[pp2][:, :]), inc="ev")
                    ea = Eacc[2 * h2 + qtt]
                    bp.emit("vector", lambda e, pp2=pp2, ea=ea: e.tensor_add(
                        out=ea, in0=ea, in1=Ech[pp2][:, :]), inc="ev")
                    ev_cons[pp2] = bp.val("ev")
                    hh = 2 * grp + h2
                    v1 = kvt[pp][:, 2048 * 2 + 128 * hh:2048 * 2 + 128 * hh + 128]
                    v2 = kvt[pp][:, 2048 * 3 + 128 * hh:2048 * 3 + 128 * hh + 128]
                    bp.wait("tensor", "ev", ev_cons[pp2])
                    def mk_av(e, cb=cb, v1=v1, v2=v2, pp2=pp2, kvc=kvc):
                        e.matmul(cb[:, 0:512], v1, a1[pp2][:, :], start=(kvc == 0), stop=False)
                        e.matmul(cb[:, 0:512], v2, a1[pp2][:, :], start=False, stop=False)
                        return e.matmul(cb[:, 0:512], v1, a2[pp2][:, :], start=False, stop=(kvc == 15))
                    bp.emit("tensor", mk_av, inc="mm")
                    av_done[pp2] = bp.val("mm")
                self._av_mm = bp.val("mm")
            ev_full = bp.val("ev")
            bp.wait("tensor", "ev", ev_full)
            for i, (h2, qtt) in enumerate(((0, 0), (0, 1), (1, 0), (1, 1))):
                bp.emit("tensor", lambda e, i=i: e.matmul(
                    self.pR[0:1, 0:512], self.onec[:, :], Eacc[i], start=True, stop=True), inc="mm")
                bp.wait("vector", "mm", drain=True)
                zoff = R_QG + 512 * i
                bp.emit("vector", lambda e, zoff=zoff: e.reciprocal(
                    out=self.rows[0:1, zoff:zoff + 512], in_=self.pR[0:1, 0:512]), inc="ev")
                self.vbubble()
                bp.wait("tensor", "ev")
            for i, (h2, qtt) in enumerate(((0, 0), (0, 1), (1, 0), (1, 1))):
                zoff = R_QG + 512 * i
                self.brow(self.rows[0:1, zoff:zoff + 512], 512, Zb, ps=self.pA)
                cb = combp[(h2, qtt)]
                bp.wait("vector", "mm", drain=True)
                bp.wait("vector", "st")
                bp.emit("vector", lambda e, cb=cb: e.tensor_mul(
                    out=cfs, in0=cb[:, 0:512], in1=Zb), inc="ev")
                bp.emit("vector", lambda e: e.tensor_copy(out=cs1, in_=cfs), inc="ev")
                bp.emit("vector", lambda e: e.tensor_sub(out=cs2, in0=cfs, in1=cs1), inc="ev")
                bp.wait("sync", "ev")
                hh = 2 * grp + h2
                for sp, tile in ((0, cs1), (1, cs2)):
                    r0 = 2048 * sp + 128 * hh
                    c0 = 512 * qtt
                    bp.emit("sync", lambda e, r0=r0, c0=c0, tile=tile: e.dma_start(
                        out=self.comb_sp[r0:r0 + 128, c0:c0 + 512], in_=tile), inc="st", n=16)

    def p3c(self):
        bp = self.bp
        wstage = self.arC[:, 2560:3584]
        wo1 = self.arB[:, 8192:10240]
        wo2 = self.arB[:, 12288:14336]
        ctxa = self.arB[:, 0:2048]
        ctxb = self.arB[:, 4096:6144]
        wov = self.wout.rearrange("p (k c) -> p k c", k=NCH)
        fop = [(self.pA, 0), (self.pA, 256), (self.pB, 0), (self.pB, 256),
               (self.pC, 0), (self.pC, 256), (self.pD, 0), (self.pD, 256)]
        for foh in range(2):
            for tN in range(4):
                bp.wait("vector", "mm")
                for p in (self.pA, self.pB, self.pC, self.pD):
                    bp.emit("vector", lambda e, p=p: e.memset(p[:, :], 0.0))
                bp.emit("vector", lambda e: e.nop(), inc="ev")
                ev_zero = bp.val("ev")
                bp.wait("tensor", "ev", ev_zero)
                for ck in range(16):
                    bp.wait("sync", "mm")
                    cslot1 = ctxa[:, 256 * (ck % 4):256 * (ck % 4) + 256]
                    cslot2 = ctxb[:, 256 * (ck % 4):256 * (ck % 4) + 256]
                    bp.emit("sync", lambda e, ck=ck, tN=tN, cslot1=cslot1: e.dma_start(
                        out=cslot1, in_=self.comb_sp[128 * ck:128 * ck + 128, 256 * tN:256 * tN + 256]),
                        inc="ld", n=16)
                    bp.emit("sync", lambda e, ck=ck, tN=tN, cslot2=cslot2: e.dma_start(
                        out=cslot2, in_=self.comb_sp[2048 + 128 * ck:2048 + 128 * ck + 128,
                                                     256 * tN:256 * tN + 256]),
                        inc="ld", n=16)
                    bp.wait("sync", "ev")
                    bp.emit("sync", lambda e, ck=ck, foh=foh: e.dma_start(
                        out=wstage, in_=wov[:, ck, 1024 * foh:1024 * foh + 1024]), inc="ld", n=16)
                    bp.wait("vector", "ld")
                    bp.wait("vector", "mm")
                    w1c = wo1[:, 0:1024]
                    w2c = wo2[:, 0:1024]
                    bp.emit("vector", lambda e, w1c=w1c: e.tensor_copy(out=w1c, in_=wstage), inc="ev")
                    bp.emit("vector", lambda e, w1c=w1c, w2c=w2c: e.tensor_sub(
                        out=w2c, in0=wstage, in1=w1c), inc="ev")
                    bp.wait("tensor", "ev")
                    bp.wait("tensor", "ld")
                    def mk_wo(e, cslot1=cslot1, cslot2=cslot2, w1c=w1c, w2c=w2c):
                        last = None
                        for fo in range(8):
                            bk, c0 = fop[fo]
                            ww1 = w1c[:, 128 * fo:128 * fo + 128]
                            ww2 = w2c[:, 128 * fo:128 * fo + 128]
                            e.matmul(bk[:, c0:c0 + 256], ww1, cslot1, start=False, stop=False,
                                     skip_group_check=True)
                            e.matmul(bk[:, c0:c0 + 256], ww2, cslot1, start=False, stop=False,
                                     skip_group_check=True)
                            last = e.matmul(bk[:, c0:c0 + 256], ww1, cslot2, start=False, stop=False,
                                            skip_group_check=True)
                        return last
                    bp.emit("tensor", mk_wo, inc="mm")
                bp.wait("vector", "mm", drain=True)
                for fo in range(8):
                    bk, c0 = fop[fo]
                    fog = 8 * foh + fo
                    xs = self.xres[:, fog * T + 256 * tN: fog * T + 256 * tN + 256]
                    bp.emit("vector", lambda e, bk=bk, c0=c0, xs=xs: e.tensor_add(
                        out=xs, in0=xs, in1=bk[:, c0:c0 + 256]), inc="ev")

    def p4pre(self):
        bp = self.bp
        slots = [(self.arB[:, 1024 * j:1024 * j + 512],
                  self.arB[:, 1024 * j + 512:1024 * j + 1024]) for j in range(8)]
        for th in range(2):
            t0 = 512 * th
            self.ln_stats(t0, 512)
            bp.wait("vector", "mm")
            for k in range(NCH):
                s1, s2 = slots[k % 8]
                if k >= 8:
                    bp.wait("vector", "st")
                self.ln_apply(k, t0, 512, self.scr[:, 0:512], self.scr[:, 0:512])
                bp.emit("vector", lambda e, s1=s1: e.tensor_copy(
                    out=s1, in_=self.scr[:, 0:512]), inc="ev")
                bp.emit("vector", lambda e, s1=s1, s2=s2: e.tensor_sub(
                    out=s2, in0=self.scr[:, 0:512], in1=s1), inc="ev")
                bp.wait("sync", "ev")
                for sp, tile in ((0, s1), (1, s2)):
                    r0 = 2048 * sp + 128 * k
                    bp.emit("sync", lambda e, th=th, r0=r0, tile=tile: e.dma_start(
                        out=self.xg_in[th][r0:r0 + 128, :], in_=tile), inc="st", n=16)
        for hh in range(2):
            bp.wait("gpsimd", "st")
            bp.emit("gpsimd", lambda e, hh=hh: e.collective_compute(
                "AllGather", ALU.bypass, replica_groups=ALL8,
                ins=[self.xg_in[hh][:, :]], outs=[self.xg_out[hh][:, :]]), inc="cc")

    def dbg_dump(self, src_ap, nrows, ncols, dst_row0=0):
        bp = self.bp
        assert ncols <= 2048
        r = 0
        while r < nrows:
            rr = min(1024 - (dst_row0 + r) if False else nrows - r, 512)
            bp.emit("sync", lambda e, r=r, rr=rr: e.dma_start(
                out=self.out[dst_row0 + r:dst_row0 + r + rr, 0:ncols],
                in_=src_ap[r:r + rr, 0:ncols]), inc="st", n=16)
            r += rr
        bp.wait("gpsimd", "st")

    def epilogue(self):
        bp = self.bp
        stage = self.arC[:, 0:2048]
        for tg in range(8):
            for k in range(NCH):
                bp.wait("tensor", "ev")
                bp.emit("tensor", lambda e, k=k, tg=tg: e.transpose(
                    self.pF[:, 0:128],
                    self.xres[:, k * T + 128 * tg: k * T + 128 * tg + 128],
                    self.idnf[:, :]), inc="mm")
                bp.wait("vector", "mm")
                bp.emit("vector", lambda e, k=k: e.tensor_copy(
                    out=stage[:, 128 * k:128 * k + 128], in_=self.pF[:, 0:128]), inc="ev")
            bp.wait("sync", "ev")
            bp.emit("sync", lambda e, tg=tg: e.dma_start(
                out=self.out[128 * tg:128 * tg + 128, :], in_=stage), inc="st", n=16)
            bp.wait("tensor", "st")
        bp.wait("gpsimd", "st")

    def finalize(self):
        nc, bp = self.nc, self.bp
        with nc.Block() as block:
            for eng in ENGS:
                fns = bp.q[eng]
                getattr(block, eng)(lambda e, fns=fns: [f(e) for f in fns])
        return nc


def _tile_pk(w):
    n = w.shape[1]
    return np.ascontiguousarray(w.reshape(16, 128, n).transpose(1, 0, 2).reshape(128, 16 * n))


def _stage_inputs(inputs):
    x = np.asarray(inputs["L0_attn_hidden_states"])
    wq = _tile_pk(np.asarray(inputs["L0_attn_Wq"])[PERM, :])
    wk = _tile_pk(np.asarray(inputs["L0_attn_Wk"])[PERM, :])
    wv = _tile_pk(np.asarray(inputs["L0_attn_Wv"])[PERM, :])
    wphi = _tile_pk(np.asarray(inputs["L0_attn_Wphi"])[PERM, :])
    wo = _tile_pk(np.asarray(inputs["L0_attn_Wo"])[:, PERM])
    in2 = _tile_pk(np.asarray(inputs["L0_ffn_input2"])[PERM][:, PERM])
    sc01 = np.stack([np.asarray(inputs["L0_ffn_scale"]).reshape(()),
                     np.asarray(inputs["L0_ffn_scale1"]).reshape(())]).reshape(1, 2).astype(np.float32)
    wqkv = _tile_pk(np.asarray(inputs["L1_attn_w_qkv"])[PERM, :])
    wout = _tile_pk(np.asarray(inputs["L1_attn_w_out"])[:, PERM])

    gpk_ = _u8view(np.asarray(inputs["L1_ffn_gate_W_packed"]))[PERMN, :].T
    upk_ = _u8view(np.asarray(inputs["L1_ffn_up_W_packed"]))[PERMN, :].T
    gsc_ = np.repeat(np.asarray(inputs["L1_ffn_gate_scales"])[PERMN, :].T, 16, axis=0)
    usc_ = np.repeat(np.asarray(inputs["L1_ffn_up_scales"])[PERMN, :].T, 16, axis=0)
    dpk_ = _u8view(np.asarray(inputs["L1_ffn_down_W_packed"]))[PERM, :].T
    dsc_ = np.repeat(np.asarray(inputs["L1_ffn_down_scales"])[PERM, :].T, 16, axis=0)

    inv_freq = 1.0 / (10000.0 ** (np.arange(0, 64, 2, dtype=np.float64) / 64.0))
    pos = np.arange(S, dtype=np.float64)
    fr = pos[:, None] * inv_freq[None, :]
    emb = np.concatenate([fr, fr], axis=-1)
    cosT = np.cos(emb).astype(np.float32)
    sinT = np.sin(emb).astype(np.float32)
    idb = np.eye(128).astype(np.float32)

    import ml_dtypes
    in_maps = []
    for c in range(8):
        b, jj = c // 2, c % 2
        xs = _tile_pk(np.ascontiguousarray(x[b, 1024 * jj:1024 * jj + 1024, :][:, PERM].T))
        m = {
            "x0T": xs.astype(np.float32),
            "wq": wq, "wk": wk, "wv": wv, "wphi": wphi, "wo": wo,
            "in2": in2, "sc01": sc01, "wqkv": wqkv, "wout": wout,
            "gpk": np.ascontiguousarray(gpk_[:, MLPS * c:MLPS * (c + 1)]),
            "gsc": np.ascontiguousarray(gsc_[:, MLPS * c:MLPS * (c + 1)]).astype(np.float32),
            "upk": np.ascontiguousarray(upk_[:, MLPS * c:MLPS * (c + 1)]),
            "usc": np.ascontiguousarray(usc_[:, MLPS * c:MLPS * (c + 1)]).astype(np.float32),
            "dpk": np.ascontiguousarray(dpk_[1024 * c:1024 * (c + 1), :]),
            "dsc": np.ascontiguousarray(dsc_[1024 * c:1024 * (c + 1), :]).astype(np.float32),
            "ropeC": np.ascontiguousarray(
                cosT[1024 * jj:1024 * jj + 1024, :].reshape(8, 128, 64).transpose(1, 0, 2).reshape(128, 512)),
            "ropeS": np.ascontiguousarray(
                sinT[1024 * jj:1024 * jj + 1024, :].reshape(8, 128, 64).transpose(1, 0, 2).reshape(128, 512)),
            "ident": idb.astype(ml_dtypes.bfloat16),
            "identf": idb,
        }
        in_maps.append(m)
    return in_maps


_NC_CACHE = {}


def _get_program(n_stages=99):
    if n_stages not in _NC_CACHE:
        mk = MK()
        mk.p0()
        if n_stages >= 1:
            mk.p1a_half(0)
            mk.p1a_half(1)
            mk.p1b()
            mk.p1c()
            mk.p2()
            mk.p3a()
            mk.p3b()
            mk.p3c()
            mk.p4pre()
        mk.epilogue()
        _NC_CACHE[n_stages] = mk.finalize()
    return _NC_CACHE[n_stages]


def kernel(**inputs):
    in_maps = _stage_inputs(inputs)
    nc = _get_program()
    res = run_bass_kernel_spmd(nc, in_maps, core_ids=list(range(8)))
    outp = np.empty((B, S, D), np.float32)
    for c in range(8):
        b, jj = c // 2, c % 2
        outp[b, 1024 * jj:1024 * jj + 1024, :][:, PERM] = res.results[c]["out"]
    return outp
